# version 21 (final)
# baseline (speedup 1.0000x reference)
import numpy as np
import ml_dtypes
from collections import defaultdict
from contextlib import ExitStack

import concourse.bass as bass
import concourse.mybir as mybir
from concourse.bass_utils import run_bass_kernel_spmd

F32 = mybir.dt.float32
BF16 = mybir.dt.bfloat16
AF = mybir.ActivationFunctionType
OP = mybir.AluOpType
AX = mybir.AxisListType

MAGIC = 12582912.0

NCORES = 8
BPC = 4
SEQ = 2048
H = 1024
I = 2048
NST = 16
HID = 4096
TT = 512
NTC = TT // 128
NTILES = BPC * (SEQ // TT)
NKC = H // 128
NIC = I // 128
NHC = HID // 128
NQ = SEQ // TT


class P:
    def __init__(self):
        self.ops = {k: [] for k in ("T", "V", "S", "Q", "G")}
        self.cnt = defaultdict(int)

    SAFE = False

    def op(self, eng, fn, deps=(), inc=0, sem=None, pre_drain=False, tag=None, rel=False):
        sem = sem or eng
        for (k, v) in deps:
            assert v <= self.cnt[k], f"dep on future token {k}:{v} (now {self.cnt[k]})"
        if self.SAFE is True or (self.SAFE and (eng in self.SAFE or (tag and tag in self.SAFE))):
            deps = tuple((k, v) for k, v in self.cnt.items() if v > 0)
        elif self.SAFE and "qT" in self.SAFE and eng == "Q" and tag:
            deps = tuple(deps) + (("T", self.cnt["T"]),) if self.cnt["T"] else deps
        tok = None
        if inc:
            self.cnt[sem] += inc
            tok = (sem, self.cnt[sem])
        self.ops[eng].append((tuple(deps), fn, inc, sem, pre_drain, rel))
        return tok


def dep(*toks):
    return tuple(t for t in toks if t is not None)


def build_program(ntiles=NTILES, safe=False, dbg=False, cut=None):
    nc = bass.Bass("TRN2", target_bir_lowering=False, detect_race_conditions=False)

    x_d = nc.dram_tensor("x", [BPC, SEQ, H], F32, kind="ExternalInput").ap()
    w_inT_d = nc.dram_tensor("w_inT", [H, 2 * I], F32, kind="ExternalInput").ap()
    cw_d = nc.dram_tensor("cw", [128, NIC * 4], F32, kind="ExternalInput").ap()
    xpB_d = nc.dram_tensor("xpB", [128, NIC * NST], F32, kind="ExternalInput").ap()
    xpC_d = nc.dram_tensor("xpC", [128, NIC * NST], F32, kind="ExternalInput").ap()
    xpD_d = nc.dram_tensor("xpD", [128, NIC], F32, kind="ExternalInput").ap()
    alog_d = nc.dram_tensor("alog", [128, NIC * NST], F32, kind="ExternalInput").ap()
    dvec_d = nc.dram_tensor("dvec", [128, NIC], F32, kind="ExternalInput").ap()
    w_outT_d = nc.dram_tensor("w_outT", [I, H], F32, kind="ExternalInput").ap()
    w1T_d = nc.dram_tensor("w1T", [H, HID], F32, kind="ExternalInput").ap()
    b1_d = nc.dram_tensor("b1", [128, NHC], F32, kind="ExternalInput").ap()
    w2T_d = nc.dram_tensor("w2T", [HID, H], F32, kind="ExternalInput").ap()
    b2_d = nc.dram_tensor("b2", [128, NKC], F32, kind="ExternalInput").ap()
    out_d = nc.dram_tensor("out", [BPC, SEQ, H], F32, kind="ExternalOutput").ap()

    s_win = nc.dram_tensor("s_win", [16, 128, NKC, 256], BF16, kind="Internal").ap()
    s_outp = nc.dram_tensor("s_outp", [8, 128, NIC, 128], BF16, kind="Internal").ap()
    s_w1 = nc.dram_tensor("s_w1", [16, 128, NKC, 256], BF16, kind="Internal").ap()
    s_w2 = nc.dram_tensor("s_w2", [4, 4, 128, 8, 256], BF16, kind="Internal").ap()

    ident_d = nc.inline_tensor(np.eye(128, dtype=ml_dtypes.bfloat16), "ident").ap()
    ones1_d = nc.inline_tensor(np.ones((1, 128), dtype=ml_dtypes.bfloat16), "ones1").ap()
    sel_np = np.zeros((16, 16, 128), dtype=ml_dtypes.bfloat16)
    for n in range(16):
        sel_np[n, n, :] = 1
    sel_d = nc.inline_tensor(sel_np.reshape(16, 16 * 128), "sel").ap()
    bias_np = np.broadcast_to(
        np.array([1e-5, 1.0, 2.772588722239781, MAGIC], dtype=np.float32), (128, 4)).copy()
    bias_d = nc.inline_tensor(bias_np, "biasc").ap()

    prog = P()
    prog.SAFE = safe
    op = prog.op
    ctx = ExitStack()
    E = ctx.enter_context
    sb = lambda name, shape, dt: E(nc.sbuf_tensor(name, shape, dt))

    x_tm = [sb(f"x_tm{i}", [128, NTC, H], BF16) for i in range(3)]
    sq_scr = sb("sq_scr", [128, 256], F32)
    q_scr = sb("q_scr", [128, H], F32)
    lntp = [sb(f"lntp{i}", [128, H], BF16) for i in range(2)]
    actF = [sb(f"actF{i}", [128, NKC, TT], BF16) for i in range(2)]
    st = [sb(f"st{i}", [128, 16], F32) for i in range(2)]
    xh_scr = [sb(f"xh{i}", [128, TT + 3], BF16) for i in range(2)]
    xc_scr = sb("xc_scr", [128, TT], BF16)
    xh_tail = sb("xh_tail", [128, NIC * 3], BF16)
    x_conv = sb("x_conv", [128, NIC, TT], BF16)
    u_sb = sb("u", [128, NIC, TT], BF16)
    g_bc = sb("g_bc", [128, NST, TT], BF16)
    delta_bc = sb("delta_bc", [128, TT], BF16)
    y_w = sb("y_w", [128, TT], F32)
    y_sb = sb("y", [128, NIC, TT], BF16)
    hq = sb("hq", [128, NHC, TT], BF16)
    o_sc = [sb(f"o_sc{i}", [128, TT], BF16) for i in range(2)]
    t_scr = [sb(f"t_scr{i}", [128, TT], F32) for i in range(2)]
    proj_sb = sb("proj_sb", [16, 3 * TT], F32)
    g_bf = sb("g_bf", [16, TT], BF16)
    delta16 = sb("delta16", [1, TT], BF16)
    winbuf = [sb(f"winbuf{i}", [128, NKC, 256], BF16) for i in range(2)]
    outpbuf = [sb(f"outpbuf{i}", [128, NIC, 128], BF16) for i in range(2)]
    w1buf = [sb(f"w1buf{i}", [128, NKC, 256], BF16) for i in range(2)]
    w2buf = [sb(f"w2buf{i}", [128, NHC // 4, 256], BF16) for i in range(2)]
    ident_sb = sb("ident_sb", [128, 128], BF16)
    ones1_sb = sb("ones1_sb", [1, 128], BF16)
    sel_sb = sb("sel_sb", [16, 16 * 128], BF16)
    cw_sb = sb("cw_sb", [128, NIC * 4], F32)
    xpB_sb = sb("xpB_sb", [128, NIC * NST], BF16)
    xpC_sb = sb("xpC_sb", [128, NIC * NST], BF16)
    xpD_sb = sb("xpD_sb", [128, NIC], BF16)
    a_sb = sb("a_sb", [128, NIC * NST], F32)
    dvec_sb = sb("dvec_sb", [128, NIC], F32)
    b1m_sb = sb("b1m_sb", [128, NHC], F32)
    b2q_sb = sb("b2q_sb", [128, NKC], F32)
    bias_sb = sb("bias_sb", [128, 4], F32)
    xp_f32 = sq_scr
    wq_in = [q_scr[:, 0:512], q_scr[:, 512:1024], t_scr[0][:], t_scr[1][:]]
    wq_out = [lntp[0][:, 0:512], lntp[0][:, 512:1024],
              lntp[1][:, 0:512], lntp[1][:, 512:1024]]

    ps = [E(nc.psum_tensor(f"ps{i}", [128, 512], F32)) for i in range(6)]
    pb = [E(nc.psum_tensor(f"pb{i}", [128, 1024], BF16)) for i in range(2)]

    sems = {k: E(nc.semaphore(f"sem_{k}")) for k in (
        "T", "V", "S", "QD", "GX", "GO", "CN", "SW", "SO", "WQ0", "WQ1", "WQ2", "WQ3",
        "WIN0", "WIN1", "OP0", "OP1", "W10", "W11", "W20", "W21",
        "GX0", "GX1", "GX2", "GO0", "GO1", "GO2")}

    ps_tok = [None] * 6
    pb_tok = [None] * 2

    for dst, src in ((ident_sb, ident_d), (ones1_sb, ones1_d), (sel_sb, sel_d),
                     (cw_sb, cw_d), (dvec_sb, dvec_d), (bias_sb, bias_d)):
        tk_const = op("G", lambda g, d=dst, s=src: g.dma_start(out=d[:], in_=s),
                      inc=16, sem="CN")
    w_inT_r = w_inT_d.rearrange("(kc p) (g2 e) -> g2 p kc e", p=128, e=256)
    for g2 in range(16):
        tk_swin = op("G", lambda g, g2=g2: g.dma_start(
            out=s_win[g2], in_=w_inT_r[g2]), inc=16, sem="SW")
    w_outT_r = w_outT_d.rearrange("(kc p) (hc e) -> hc p kc e", p=128, e=128)
    for hc in range(8):
        tk_soutp = op("G", lambda g, hc=hc: g.dma_start(
            out=s_outp[hc], in_=w_outT_r[hc]), inc=16, sem="SO")

    tk = op("Q", lambda q: q.dma_start(out=xp_f32[:, 0:NIC * NST], in_=xpB_d), inc=16, sem="QD")
    tk_xpB = op("V", lambda v: v.tensor_copy(xpB_sb[:], xp_f32[:, 0:NIC * NST]), deps=(tk,), inc=1)
    tk = op("Q", lambda q: q.dma_start(out=xp_f32[:, 0:NIC * NST], in_=xpC_d), deps=(tk_xpB,), inc=16, sem="QD")
    tk_xpC = op("V", lambda v: v.tensor_copy(xpC_sb[:], xp_f32[:, 0:NIC * NST]), deps=(tk,), inc=1)
    tk = op("Q", lambda q: q.dma_start(out=xp_f32[:, 0:NIC * NST], in_=alog_d), deps=(tk_xpC,), inc=16, sem="QD")
    tk_alog = op("S", lambda s: s.activation(a_sb[:], xp_f32[:, 0:NIC * NST], AF.Exp), deps=(tk,), inc=1)
    tk_a = op("V", lambda v: v.tensor_scalar(a_sb[:], a_sb[:], -1.0, None, OP.mult),
              deps=(tk_alog,), inc=1)
    tk = op("Q", lambda q: q.dma_start(out=xp_f32[:, 0:NIC], in_=xpD_d), deps=(tk_alog,), inc=16, sem="QD")
    tk_xpD = op("V", lambda v: v.tensor_copy(xpD_sb[:], xp_f32[:, 0:NIC]), deps=(tk,), inc=1)

    tk = op("Q", lambda q: q.dma_start(out=xp_f32[:, 0:NHC], in_=b1_d), deps=(tk_xpD,), inc=16, sem="QD")
    op("V", lambda v: v.tensor_scalar(b1m_sb[:], xp_f32[:, 0:NHC], 64.0, MAGIC, OP.mult, OP.add), deps=(tk,))
    op("V", lambda v: v.tensor_scalar(b1m_sb[:], b1m_sb[:], MAGIC + 127.0, MAGIC - 128.0, OP.min, OP.max), pre_drain=True)
    tk_b1 = op("V", lambda v: v.tensor_scalar(b1m_sb[:], b1m_sb[:], MAGIC, 0.25, OP.subtract, OP.mult), inc=1, pre_drain=True)
    tk = op("Q", lambda q: q.dma_start(out=xp_f32[:, 0:NKC], in_=b2_d), deps=(tk_b1,), inc=16, sem="QD")
    op("V", lambda v: v.tensor_scalar(b2q_sb[:], xp_f32[:, 0:NKC], 64.0, MAGIC, OP.mult, OP.add), deps=(tk,))
    op("V", lambda v: v.tensor_scalar(b2q_sb[:], b2q_sb[:], MAGIC + 127.0, MAGIC - 128.0, OP.min, OP.max), pre_drain=True)
    tk_b2 = op("V", lambda v: v.tensor_scalar(b2q_sb[:], b2q_sb[:], MAGIC, 1.0 / 64.0, OP.subtract, OP.mult), inc=1, pre_drain=True)

    wq_free = [tk_b2, tk_b2, tk_b2, tk_b2]
    st_tok = [None, None, None, None]
    quant_jobs = []
    for kc in range(NKC):
        for cc in range(8):
            quant_jobs.append((w1T_d, s_w1, kc, cc * 512))
    for kc in range(NHC):
        for cc in range(2):
            quant_jobs.append((w2T_d, s_w2, kc, cc * 512))
    for j, (srcd, dstd, kc, co) in enumerate(quant_jobs):
        bi = j % 4
        tk_l = op("Q", lambda q, s=srcd, kc=kc, co=co, b=bi:
                  q.dma_start(out=wq_in[b], in_=s[kc * 128:(kc + 1) * 128, co:co + 512]),
                  deps=dep(wq_free[bi]), inc=16, sem=f"WQ{bi}")
        tk_s1 = op("S", lambda s, b=bi: s.activation(wq_in[b], wq_in[b], AF.Identity,
                                                     scale=64.0, bias=bias_sb[:, 3:4]),
                   deps=dep(tk_l, st_tok[bi], tk_const), inc=1, rel=True)
        op("V", lambda v, b=bi: v.tensor_scalar(wq_in[b], wq_in[b], MAGIC + 127.0, MAGIC - 128.0, OP.min, OP.max),
           deps=(tk_s1,))
        tk_q = op("V", lambda v, b=bi: v.tensor_scalar(wq_out[b], wq_in[b], MAGIC, None, OP.subtract), inc=1)
        wq_free[bi] = tk_q
        if dstd is s_w1:
            dst_ap = s_w1[co // 256:co // 256 + 2, :, kc, :]
        else:
            dst_ap = s_w2[co // 256:co // 256 + 2, kc // 8, :, kc % 8, :]
        st_tok[bi] = op("G", lambda g, d=dst_ap.rearrange("bb p e -> p bb e"), b=bi:
                        g.dma_start(out=d, in_=wq_out[b]),
                        deps=(tk_q,), inc=16, sem="GX")
    tk_scratch = ("GX", prog.cnt["GX"])
    tk_p0v = ("V", prog.cnt["V"])

    store_tok = [None] * ntiles
    win_free = [None, None]
    outp_free = [None, None]
    w1_free = [None, None]
    w2_free = [None, None]
    xh_free = [None, None]
    osc_free = [None, None]
    tscr_free = [tk_p0v, tk_p0v]
    xtm_free = [None, None, None]
    tp_done_par = [None, None]
    aF_par = [tk_p0v, tk_p0v]
    ffn1_lastT_par = [None, None]
    gbc_last_v = None
    ytile_last = None
    hq_last = None
    hq_ready_prev = None
    projrd_last = None
    TS = [dict() for _ in range(ntiles)]

    EPb = [hq[:, 0:NST, :], hq[:, NST:2 * NST, :]]

    def emit_ln_stats(ti, dep0, scale16=False):
        par, pt = ti % 2, ti % 3
        stp = st[par]
        sq0 = 48 if scale16 else 0
        for tc in range(NTC):
            op("V", lambda v, p=pt, tc=tc, sq0=sq0: v.bn_stats(
                sq_scr[:, sq0 + 12 * tc:sq0 + 12 * tc + 6], x_tm[p][:, tc, 0:512]),
                deps=dep0 if tc == 0 else ())
            op("V", lambda v, p=pt, tc=tc, sq0=sq0: v.bn_stats(
                sq_scr[:, sq0 + 12 * tc + 6:sq0 + 12 * tc + 12], x_tm[p][:, tc, 512:1024]))
        for tc in range(NTC):
            tk_var = op("V", lambda v, tc=tc, stp=stp, sq0=sq0: v.bn_aggr(
                stp[:, 2 * tc:2 * tc + 2], sq_scr[:, sq0 + 12 * tc:sq0 + 12 * tc + 12]),
                pre_drain=(tc == 0), inc=1 if tc == NTC - 1 else 0)
        op("S", lambda s, stp=stp: s.activation(
            stp[:, 8:12], stp[:].rearrange("p (a b) -> p a b", b=2)[:, 0:4, 1], AF.Ln,
            bias=bias_sb[:, 0:1]), deps=dep(tk_var, tk_const if ti == 0 else None))
        if scale16:
            tk_rstd = op("S", lambda s, stp=stp: s.activation(
                stp[:, 8:12], stp[:, 8:12], AF.Exp, scale=-0.5, bias=bias_sb[:, 2:3]),
                inc=1, pre_drain=True)
        else:
            tk_rstd = op("S", lambda s, stp=stp: s.activation(
                stp[:, 8:12], stp[:, 8:12], AF.Exp, scale=-0.5), inc=1, pre_drain=True)
        return tk_rstd


    def emit_load_A(ti):
        S = TS[ti]
        seq, qq, par = ti // NQ, ti % NQ, ti % 2
        pt = ti % 3
        s0 = qq * TT
        tk_x = op("G", lambda g, b=seq, s0=s0, p=pt: g.dma_start(
            out=x_tm[p][:],
            in_=x_d[b, s0:s0 + TT, :].rearrange("(tc p) h -> p tc h", p=128)),
            deps=dep(xtm_free[pt]), inc=16, sem=f"GX{pt}")
        stp, lnp = st[par], lntp[par]
        tk_rstd = emit_ln_stats(ti, dep(tk_x, aF_par[par]))
        aF_tok = None
        for tc in range(NTC):
            tk_ln = op("V", lambda v, p=pt, tc=tc, stp=stp, lnp=lnp: v.tensor_scalar(
                lnp[:], x_tm[p][:, tc, :], stp[:, 2 * tc:2 * tc + 1], stp[:, 8 + tc:9 + tc],
                OP.subtract, OP.mult),
                deps=dep(tk_rstd, tp_done_par[par],
                         tk_scratch if ti <= 1 and tc == 0 else None), inc=1,
                pre_drain=(tc == 0))
            pbi = tc % 2
            for hc in range(NKC):
                tkt = op("T", lambda t, hc=hc, pbi=pbi, lnp=lnp: t.transpose(
                    pb[pbi][:, hc * 128:(hc + 1) * 128], lnp[:, hc * 128:(hc + 1) * 128], ident_sb[:]),
                    deps=dep(tk_ln, pb_tok[pbi], tk_const if ti == 0 else None) if hc == 0 else (),
                    inc=1 if hc == NKC - 1 else 0)
            tp_done_par[par] = tkt
            aF_tok = op("S", lambda s, tc=tc, pbi=pbi, p=par: s.copy(
                actF[p][:, :, tc * 128:(tc + 1) * 128],
                pb[pbi][:].rearrange("p (kc t) -> p kc t", kc=NKC)),
                deps=dep(tkt, ffn1_lastT_par[par] if tc == 0 else None), inc=1, rel=(tc < NTC - 1))
            pb_tok[pbi] = aF_tok
        S["aF"] = aF_tok
        S["win_tok"] = {0: _win_dma(ti, 0), 1: _win_dma(ti, 1)}
        S["mm"] = {}
        S["zev"] = {}

    def _win_dma(ti, g2):
        bi = g2 % 2
        return op("Q", lambda q, g2=g2, b=bi: q.dma_start(
            out=winbuf[b][:], in_=s_win[g2]),
            deps=dep(win_free[bi], tk_swin if ti == 0 else None), inc=16, sem=f"WIN{bi}", tag="qwin")

    def emit_B_pair(ti, g2):
        S = TS[ti]
        qq, par = ti % NQ, ti % 2
        bi = g2 % 2
        for oc2 in range(2):
            oc = 2 * g2 + oc2
            psb = oc % 3
            for kc in range(NKC):
                d = dep(S["win_tok"][g2], S["aF"], ps_tok[psb]) if kc == 0 else ()
                tkm = op("T", lambda t, b=bi, kc=kc, oc2=oc2, psb=psb, p=par: t.matmul(
                    ps[psb][:], winbuf[b][:, kc, oc2 * 128:(oc2 + 1) * 128],
                    actF[p][:, kc, :], start=(kc == 0), stop=(kc == NKC - 1)),
                    deps=d, inc=1 if kc == NKC - 1 else 0)
            S["mm"][oc] = tkm
            if oc < NIC:
                c = oc
                xp = c % 2

                def emit_head(hc):
                    if qq == 0:
                        op("V", lambda v, xp=hc % 2: v.memset(xh_scr[xp][:, 0:3], 0.0),
                           deps=dep(tk_const if ti == 0 else None))
                    else:
                        op("V", lambda v, hc=hc, xp=hc % 2: v.tensor_copy(
                            xh_scr[xp][:, 0:3], xh_tail[:, hc * 3:(hc + 1) * 3]))

                if c == 0:
                    emit_head(0)
                    emit_head(1)
                tk_ev = op("V", lambda v, xp=xp, psb=psb: v.tensor_copy(xh_scr[xp][:, 3:3 + TT], ps[psb][:]),
                           deps=dep(S["mm"][oc],
                                    TS[ti - 1]["xproj"] if (ti > 0 and c == 0) else None),
                           inc=1, rel=True)
                ps_tok[psb] = tk_ev
                op("V", lambda v, c=c, xp=xp: v.tensor_scalar(
                    xc_scr[:], xh_scr[xp][:, 0:TT], cw_sb[:, 4 * c:4 * c + 1], None, OP.mult))
                for k in range(1, 3):
                    op("V", lambda v, c=c, xp=xp, k=k: v.scalar_tensor_tensor(
                        xc_scr[:], xh_scr[xp][:, k:k + TT], cw_sb[:, 4 * c + k:4 * c + k + 1],
                        xc_scr[:], OP.mult, OP.add))
                op("V", lambda v, c=c, xp=xp: v.scalar_tensor_tensor(
                    x_conv[:, c, :], xh_scr[xp][:, 3:3 + TT], cw_sb[:, 4 * c + 3:4 * c + 4],
                    xc_scr[:], OP.mult, OP.add))
                if c + 2 < NIC:
                    emit_head(c + 2)
                op("V", lambda v, c=c, xp=xp: v.tensor_copy(
                    xh_tail[:, c * 3:(c + 1) * 3], xh_scr[xp][:, TT:TT + 3]))
            else:
                c = oc - NIC
                if oc == 31:
                    S["z_def"] = psb
                else:
                    tk_ev = op("V", lambda v, c=c, psb=psb: v.tensor_copy(u_sb[:, c, :], ps[psb][:]),
                               deps=dep(S["mm"][oc]), inc=1, rel=True)
                    ps_tok[psb] = tk_ev
                    S["zev"][c] = tk_ev
        win_free[bi] = S["mm"][2 * g2 + 1]
        if g2 + 2 < 16:
            S["win_tok"][g2 + 2] = _win_dma(ti, g2 + 2)
        if g2 == 15:
            S["inproj_lastT"] = S["mm"][31]

    def emit_silus(ti):
        S = TS[ti]
        tk = None
        for c in range(NIC):
            tk = op("S", lambda s, c=c: s.activation(x_conv[:, c, :], x_conv[:, c, :], AF.Silu),
                    deps=dep(S["bdone"], TS[ti - 1]["yv"] if ti > 0 else None) if c == 0 else (),
                    inc=1 if c == NIC - 1 else 0, rel=True)
        S["silux"] = tk
        for c in range(NIC):
            tk = op("S", lambda s, c=c: s.activation(u_sb[:, c, :], u_sb[:, c, :], AF.Silu),
                    inc=1 if c == NIC - 1 else 0, rel=True)
        S["siluu"] = tk

    def emit_umults(ti):
        S = TS[ti]
        for c in range(NIC):
            op("V", lambda v, c=c: v.tensor_tensor(
                u_sb[:, c, :], u_sb[:, c, :], x_conv[:, c, :], OP.mult),
                deps=(S["siluu"],) if c == 0 else ())

    def emit_C(ti):
        nonlocal projrd_last, gbc_last_v
        S = TS[ti]
        for c in range(NIC):
            d0 = dep(S["silux"], ps_tok[3], ps_tok[4], ps_tok[5]) if c == 0 else ()
            op("T", lambda t, c=c: t.matmul(
                ps[3][0:16, :], xpB_sb[:, c * NST:(c + 1) * NST], x_conv[:, c, :],
                start=(c == 0), stop=(c == NIC - 1)), deps=d0)
            op("T", lambda t, c=c: t.matmul(
                ps[4][0:16, :], xpC_sb[:, c * NST:(c + 1) * NST], x_conv[:, c, :],
                start=(c == 0), stop=(c == NIC - 1)))
            tk_xproj = op("T", lambda t, c=c: t.matmul(
                ps[5][0:1, :], xpD_sb[:, c:c + 1], x_conv[:, c, :],
                start=(c == 0), stop=(c == NIC - 1)), inc=1 if c == NIC - 1 else 0)
        S["xproj"] = tk_xproj
        tk_e0 = op("S", lambda s: s.copy(proj_sb[:, 0:TT], ps[3][0:16, :]),
                   deps=dep(tk_xproj, projrd_last), inc=1)
        tk_e1 = op("S", lambda s: s.copy(proj_sb[:, TT:2 * TT], ps[4][0:16, :]), inc=1)
        tk_e2 = op("S", lambda s: s.copy(proj_sb[0:1, 2 * TT:3 * TT], ps[5][0:1, :]), inc=1)
        ps_tok[3], ps_tok[4], ps_tok[5] = tk_e0, tk_e1, tk_e2
        tk_gbf = op("V", lambda v: v.tensor_tensor(g_bf[:], proj_sb[:, 0:TT], proj_sb[:, TT:2 * TT], OP.mult),
                    deps=dep(tk_e1, projrd_last), inc=1)
        tk_sp0 = op("V", lambda v: v.tensor_scalar(y_w[0:1, :], proj_sb[0:1, 2 * TT:3 * TT], 80.0, None, OP.min),
                    deps=(tk_e2,), inc=1)
        op("S", lambda s: s.activation(q_scr[0:1, 0:TT], y_w[0:1, :], AF.Exp), deps=(tk_sp0,))
        tk_d16 = op("S", lambda s: s.activation(delta16[:], q_scr[0:1, 0:TT], AF.Ln,
                                                bias=bias_sb[0:1, 1:2]), inc=1)
        tk_mm = op("T", lambda t: t.matmul(ps[3][:], ones1_sb[:], delta16[:], start=True, stop=True),
                   deps=dep(tk_d16, ps_tok[3], tk_const if ti == 0 else None), inc=1)
        tk_dbc = op("S", lambda s: s.copy(delta_bc[:], ps[3][:]), deps=(tk_mm,), inc=1, rel=True)
        ps_tok[3] = tk_dbc
        for n in range(NST):
            psb = 3 + (n + 1) % 2
            tk_mm = op("T", lambda t, n=n, psb=psb: t.matmul(
                ps[psb][:], sel_sb[:, n * 128:(n + 1) * 128], g_bf[:], start=True, stop=True),
                deps=dep(tk_gbf if n == 0 else None, ps_tok[psb]), inc=1)
            tk_gev = op("S", lambda s, n=n, psb=psb: s.copy(g_bc[:, n, :], ps[psb][:]),
                        deps=dep(tk_mm, gbc_last_v if n == 0 else None), inc=1,
                        rel=True)
            ps_tok[psb] = tk_gev
        S["gbc"] = tk_gev
        S["dbc"] = tk_dbc
        projrd_last = tk_mm
        S["ep_free"] = [hq_last, hq_last]
        S["yv"] = None
        S["outp_pre"] = {0: _outp_dma(ti, 0), 1: _outp_dma(ti, 1)}
        S["w1_pre"] = {0: _w1_dma(ti, 0), 1: _w1_dma(ti, 1)}
        S["w2_pre"] = {(0, hh): _w2_dma(ti, 0, hh) for hh in range(2)}

    def emit_D_chunk(ti, c):
        nonlocal gbc_last_v
        S = TS[ti]
        EP = EPb[c % 2]
        for n in range(NST):
            d = dep(S["dbc"], tk_a if ti == 0 else None, S["ep_free"][c % 2],
                    hq_ready_prev if (ti > 0 and c <= 1) else None) if n == 0 else ()
            tk_exp = op("S", lambda s, n=n, c=c, EP=EP: s.activation(
                EP[:, n, :], delta_bc[:], AF.Exp,
                scale=a_sb[:, c * NST + n:c * NST + n + 1]),
                deps=d, inc=1 if n == NST - 1 else 0, rel=True)
        S["exp%d" % c] = tk_exp
        op("V", lambda v, EP=EP: v.tensor_tensor(EP[:, :, :], EP[:, :, :], g_bc[:, :, :], OP.mult),
           deps=dep(tk_exp, S["gbc"] if c == 0 else None))
        op("V", lambda v, EP=EP: v.tensor_tensor(
            EP[:, 0:8, :], EP[:, 0:8, :], EP[:, 8:16, :], OP.add))
        op("V", lambda v, EP=EP: v.tensor_tensor(
            EP[:, 0:4, :], EP[:, 0:4, :], EP[:, 4:8, :], OP.add))
        op("V", lambda v, EP=EP: v.tensor_tensor(
            EP[:, 0:2, :], EP[:, 0:2, :], EP[:, 2:4, :], OP.add))
        op("V", lambda v, EP=EP: v.tensor_tensor(y_w[:], EP[:, 0, :], EP[:, 1, :], OP.add))
        tk_y = op("V", lambda v, c=c: v.scalar_tensor_tensor(
            y_sb[:, c, :], y_w[:], dvec_sb[:, c:c + 1], u_sb[:, c, :], OP.add, OP.mult),
            deps=dep(ytile_last if c == 0 else None), inc=1, rel=True)
        S["ep_free"][c % 2] = tk_y
        if c == NIC - 1:
            S["yv"] = tk_y
            gbc_last_v = tk_y

    def emit_D2(ti):
        nonlocal ytile_last
        S = TS[ti]
        par, pt = ti % 2, ti % 3
        outp_tok = S.pop("outp_pre")
        for blk in range(4):
            ev_tok = []
            for hcp in range(2):
                hc = 2 * blk + hcp
                bi = hc % 2
                psb = 3 + hcp
                for kc in range(NIC):
                    d = dep(outp_tok[hc], S["yv"], ps_tok[psb]) if kc == 0 else ()
                    tk_mm = op("T", lambda t, b=bi, kc=kc, psb=psb: t.matmul(
                        ps[psb][:], outpbuf[b][:, kc, :],
                        y_sb[:, kc, :], start=(kc == 0), stop=(kc == NIC - 1)),
                        deps=d, inc=1 if kc == NIC - 1 else 0)
                osl = hcp
                tk_ev = op("V", lambda v, osl=osl, psb=psb: v.tensor_copy(o_sc[osl][:], ps[psb][:]),
                           deps=dep(tk_mm, osc_free[osl]), inc=1, rel=(hcp == 0))
                ps_tok[psb] = tk_ev
                outp_free[bi] = tk_ev
                ev_tok.append(tk_ev)
                if hc + 2 < 8:
                    outp_tok[hc + 2] = _outp_dma(ti, hc + 2)
            if blk == 3:
                ytile_last = ps_tok[4]
            pbi = blk % 2
            for tc in range(NTC):
                for hcp in range(2):
                    tk_tp = op("T", lambda t, tc=tc, hcp=hcp, pbi=pbi: t.transpose(
                        pb[pbi][:, tc * 256 + hcp * 128: tc * 256 + (hcp + 1) * 128],
                        o_sc[hcp][:, tc * 128:(tc + 1) * 128], ident_sb[:]),
                        deps=dep(ev_tok[-1], pb_tok[pbi]) if tc == 0 else (),
                        inc=1 if (tc == NTC - 1 and hcp == 1) else 0)
            osc_free[0] = osc_free[1] = tk_tp
            for tc in range(NTC):
                add_tok = op("V", lambda v, p=pt, tc=tc, blk=blk, pbi=pbi: v.tensor_tensor(
                    x_tm[p][:, tc, blk * 256:(blk + 1) * 256],
                    x_tm[p][:, tc, blk * 256:(blk + 1) * 256],
                    pb[pbi][:, tc * 256:(tc + 1) * 256], OP.add),
                    deps=(tk_tp,) if tc == 0 else (), inc=1 if tc == NTC - 1 else 0,
                    rel=True)
            pb_tok[pbi] = add_tok
        S["attn_add"] = add_tok

    def _outp_dma(ti, hc):
        bi = hc % 2
        return op("Q", lambda q, hc=hc, b=bi: q.dma_start(
            out=outpbuf[b][:], in_=s_outp[hc]),
            deps=dep(outp_free[bi], tk_soutp if ti == 0 else None), inc=16, sem=f"OP{bi}", tag="qoutp")

    def emit_E(ti):
        S = TS[ti]
        par, pt = ti % 2, ti % 3
        stp, lnp = st[par], lntp[par]
        add_tok = S["attn_add"]
        tk_rstd = emit_ln_stats(ti, dep(add_tok), scale16=True)
        aF_tok = None
        for tc in range(NTC):
            op("V", lambda v, p=pt, tc=tc, stp=stp: v.tensor_scalar(
                q_scr[:], x_tm[p][:, tc, :], stp[:, 2 * tc:2 * tc + 1], stp[:, 8 + tc:9 + tc],
                OP.subtract, OP.mult), deps=dep(tk_rstd, tp_done_par[par]), pre_drain=(tc == 0))
            op("V", lambda v: v.tensor_scalar(q_scr[:], q_scr[:], MAGIC, MAGIC + 127.0, OP.add, OP.min))
            tk_ln = op("V", lambda v, lnp=lnp: v.tensor_scalar(
                lnp[:], q_scr[:], MAGIC - 128.0, MAGIC, OP.max, OP.subtract), inc=1)
            pbi = tc % 2
            for hc in range(NKC):
                tkt = op("T", lambda t, hc=hc, pbi=pbi, lnp=lnp: t.transpose(
                    pb[pbi][:, hc * 128:(hc + 1) * 128], lnp[:, hc * 128:(hc + 1) * 128], ident_sb[:]),
                    deps=dep(tk_ln, pb_tok[pbi]) if hc == 0 else (),
                    inc=1 if hc == NKC - 1 else 0)
            tp_done_par[par] = tkt
            aF_tok = op("S", lambda s, tc=tc, pbi=pbi, p=par: s.copy(
                actF[p][:, :, tc * 128:(tc + 1) * 128],
                pb[pbi][:].rearrange("p (kc t) -> p kc t", kc=NKC)),
                deps=dep(tkt, S["inproj_lastT"] if tc == 0 else None), inc=1, rel=(tc < NTC - 1))
            pb_tok[pbi] = aF_tok
        S["xqF"] = aF_tok

    def _w1_dma(ti, blk):
        bi = blk % 2
        return op("Q", lambda q, blk=blk, b=bi: q.dma_start(
            out=w1buf[b][:], in_=s_w1[blk]),
            deps=dep(w1_free[bi], tk_scratch if ti == 0 else None), inc=16, sem=f"W1{bi}", tag="qw1")

    def _w2_dma(ti, ocp, hh):
        bi = hh % 2
        return op("Q", lambda q, ocp=ocp, hh=hh, b=bi: q.dma_start(
            out=w2buf[b][:], in_=s_w2[ocp, hh]),
            deps=dep(w2_free[bi], tk_scratch if ti == 0 else None), inc=16, sem=f"W2{bi}", tag="qw2")

    def emit_F(ti):
        nonlocal hq_last, hq_ready_prev
        S = TS[ti]
        par, pt = ti % 2, ti % 3
        aF_tok = S["xqF"]
        w1_tok = S.pop("w1_pre")
        for blk in range(16):
            bi = blk % 2
            ev2 = []
            for hcp in range(2):
                hc = 2 * blk + hcp
                psb = 3 + (2 * blk + hcp) % 3
                for kc in range(NKC):
                    d = dep(w1_tok[blk], aF_tok, ps_tok[psb]) if kc == 0 else ()
                    tk_mm = op("T", lambda t, b=bi, kc=kc, hcp=hcp, psb=psb, p=par: t.matmul(
                        ps[psb][:], w1buf[b][:, kc, hcp * 128:(hcp + 1) * 128],
                        actF[p][:, kc, :], start=(kc == 0), stop=(kc == NKC - 1)),
                        deps=d, inc=1 if kc == NKC - 1 else 0)
                tk_r = op("S", lambda s, hc=hc, psb=psb, tsp=hcp: s.activation(
                    t_scr[tsp][:], ps[psb][:], AF.Relu, scale=1.0 / 64.0,
                    bias=b1m_sb[:, hc:hc + 1]),
                    deps=dep(tk_mm, tk_b1 if ti == 0 else None,
                             hq_last if (blk == 0 and hcp == 0) else None,
                             tscr_free[hcp]), inc=1, rel=True)
                op("V", lambda v, tsp=hcp: v.tensor_scalar(
                    t_scr[tsp][:], t_scr[tsp][:], MAGIC, MAGIC + 127.0, OP.add, OP.min),
                    deps=(tk_r,))
                tk_q = op("V", lambda v, hc=hc, tsp=hcp: v.tensor_scalar(
                    hq[:, hc, :], t_scr[tsp][:], MAGIC, MAGIC, OP.max, OP.subtract), inc=1,
                    rel=(blk < 15 or hcp < 1))
                tscr_free[hcp] = tk_q
                ps_tok[psb] = tk_q
                ev2.append(tk_q)
            w1_free[bi] = tk_mm
            if blk + 2 < 16:
                w1_tok[blk + 2] = _w1_dma(ti, blk + 2)
        ffn1_lastT_par[par] = tk_mm
        hq_ready = tk_q
        hq_ready_prev = tk_q

        w2_tok = S.pop("w2_pre")
        for ocp in range(4):
            ev_tok = []
            last_mm = None
            for hh in range(4):
                bi = hh % 2
                for oc2 in range(2):
                    psb = 3 + oc2
                    for k16 in range(8):
                        hc = hh * 8 + k16
                        d = dep(w2_tok[(ocp, hh)] if (oc2 == 0 and k16 == 0) else None,
                                hq_ready if (hh == 0 and oc2 == 0 and k16 == 0) else None,
                                ps_tok[psb] if (hh == 0 and k16 == 0) else None) if k16 == 0 else ()
                        last_mm = op("T", lambda t, b=bi, k16=k16, oc2=oc2, psb=psb, hc=hc: t.matmul(
                            ps[psb][:], w2buf[b][:, k16, oc2 * 128:(oc2 + 1) * 128],
                            hq[:, hc, :], start=(hc == 0), stop=(hc == NHC - 1)),
                            deps=d,
                            inc=1 if (k16 == 7 and (oc2 == 1 or hh == 3)) else 0)
                w2_free[bi] = last_mm
                nxt = (ocp, hh + 2) if hh + 2 < 4 else ((ocp + 1, hh - 2) if ocp + 1 < 4 else None)
                if nxt is not None and nxt not in w2_tok:
                    w2_tok[nxt] = _w2_dma(ti, nxt[0], nxt[1])
            for oc2 in range(2):
                oc = 2 * ocp + oc2
                psb = 3 + oc2
                tk_ev = op("V", lambda v, oc=oc, osl=oc2, psb=psb: v.tensor_scalar(
                    o_sc[osl][:], ps[psb][:], 1.0 / 1024.0, b2q_sb[:, oc:oc + 1], OP.mult, OP.add),
                    deps=dep(last_mm, osc_free[oc2], tk_b2 if ti == 0 else None), inc=1,
                    rel=(oc2 == 0))
                ps_tok[psb] = tk_ev
                ev_tok.append(tk_ev)
            if ocp == 3:
                hq_last = ps_tok[4]
            pbi = ocp % 2
            for tc in range(NTC):
                for oc2 in range(2):
                    tk_tp = op("T", lambda t, tc=tc, oc2=oc2, pbi=pbi: t.transpose(
                        pb[pbi][:, tc * 256 + oc2 * 128: tc * 256 + (oc2 + 1) * 128],
                        o_sc[oc2][:, tc * 128:(tc + 1) * 128], ident_sb[:]),
                        deps=dep(ev_tok[-1], pb_tok[pbi]) if tc == 0 else (),
                        inc=1 if (tc == NTC - 1 and oc2 == 1) else 0)
            osc_free[0] = osc_free[1] = tk_tp
            for tc in range(NTC):
                add_tok2 = op("V", lambda v, p=pt, tc=tc, ocp=ocp, pbi=pbi: v.tensor_tensor(
                    x_tm[p][:, tc, ocp * 256:(ocp + 1) * 256],
                    x_tm[p][:, tc, ocp * 256:(ocp + 1) * 256],
                    pb[pbi][:, tc * 256:(tc + 1) * 256], OP.add),
                    deps=(tk_tp,) if tc == 0 else (), inc=1 if tc == NTC - 1 else 0,
                    rel=(ocp < 3))
            pb_tok[pbi] = add_tok2
        S["ffn_add"] = add_tok2

    def emit_store(ti):
        seq, qq, pt = ti // NQ, ti % NQ, ti % 3
        s0 = qq * TT
        store_tok[ti] = op("G", lambda g, b=seq, s0=s0, p=pt: g.dma_start(
            out=out_d[b, s0:s0 + TT, :].rearrange("(tc p) h -> p tc h", p=128),
            in_=x_tm[p][:]),
            deps=(TS[ti]["ffn_add"],), inc=16, sem=f"GO{pt}")
        xtm_free[pt] = store_tok[ti]

    for s in range(ntiles + 1):
        a, b = s, s - 1
        if b >= 0:
            emit_C(b)
        if a < ntiles:
            emit_load_A(a)
        for k in range(16):
            if a < ntiles:
                emit_B_pair(a, k)
            if b >= 0:
                emit_D_chunk(b, k)
        if a < ntiles:
            S = TS[a]
            psb = S.pop("z_def")
            tk_ev = op("V", lambda v, psb=psb: v.tensor_copy(u_sb[:, NIC - 1, :], ps[psb][:]),
                       deps=dep(S["mm"][31]), inc=1)
            ps_tok[psb] = tk_ev
            S["bdone"] = tk_ev
            emit_silus(a)
        if b >= 0:
            emit_D2(b)
        if a < ntiles:
            emit_umults(a)
        if b >= 0:
            emit_E(b)
            emit_F(b)
            emit_store(b)

    op("G", lambda g: [g.wait_ge(sems[k], prog.cnt[k]) for k in ("GO0", "GO1", "GO2", "GO") if prog.cnt[k] > 0][-1])

    prog.emitted = {k: [] for k in prog.ops}
    with nc.Block() as block:
        def emitter(letter):
            def f(eng):
                waited = defaultdict(int)
                for deps, fn, inc, sem, pre_drain, rel in prog.ops[letter]:
                    for (k, v) in deps:
                        if k == letter and k in ("T", "V", "S"):
                            continue
                        if v > waited[k]:
                            eng.wait_ge(sems[k], v)
                            waited[k] = v
                    if pre_drain:
                        eng.drain()
                    n0 = nc.n_instructions()
                    inst = fn(eng)
                    if inc:
                        if letter in ("V", "S") and not rel:
                            eng.drain().then_inc(sems[sem], inc)
                        else:
                            inst.then_inc(sems[sem], inc)
                    prog.emitted[letter].append((n0, nc.n_instructions()))
            return f
        block.tensor(emitter("T"))
        block.vector(emitter("V"))
        block.scalar(emitter("S"))
        block.sync(emitter("Q"))
        block.gpsimd(emitter("G"))

    ctx.close()
    nc._prog = prog
    return nc


_prog_cache = {}


def _get_nc():
    if "nc" not in _prog_cache:
        _prog_cache["nc"] = build_program()
    return _prog_cache["nc"]


def kernel(L0_attn_x, L0_attn_in_proj_w, L0_attn_conv_w, L0_attn_x_proj_w,
           L0_attn_A_log, L0_attn_D, L0_attn_out_proj_w,
           L0_ffn_w1, L0_ffn_b1, L0_ffn_w2, L0_ffn_b2):
    x = np.ascontiguousarray(np.asarray(L0_attn_x, dtype=np.float32))
    w_in = np.asarray(L0_attn_in_proj_w, dtype=np.float32)
    conv_w = np.asarray(L0_attn_conv_w, dtype=np.float32)
    xp = np.asarray(L0_attn_x_proj_w, dtype=np.float32)
    alog = np.asarray(L0_attn_A_log, dtype=np.float32)
    dv = np.asarray(L0_attn_D, dtype=np.float32)
    w_out = np.asarray(L0_attn_out_proj_w, dtype=np.float32)
    w1 = np.asarray(L0_ffn_w1, dtype=np.float32)
    b1 = np.asarray(L0_ffn_b1, dtype=np.float32)
    w2 = np.asarray(L0_ffn_w2, dtype=np.float32)
    b2 = np.asarray(L0_ffn_b2, dtype=np.float32)

    w_inT = np.ascontiguousarray(w_in.T)
    cw = np.ascontiguousarray(conv_w[:, 0, :].reshape(NIC, 128, 4).transpose(1, 0, 2).reshape(128, NIC * 4))
    xpB = np.ascontiguousarray(xp[1:17].T.reshape(NIC, 128, NST).transpose(1, 0, 2).reshape(128, NIC * NST))
    xpC = np.ascontiguousarray(xp[17:33].T.reshape(NIC, 128, NST).transpose(1, 0, 2).reshape(128, NIC * NST))
    xpD = np.ascontiguousarray(xp[0].reshape(NIC, 128).T)
    alog_r = np.ascontiguousarray(alog.reshape(NIC, 128, NST).transpose(1, 0, 2).reshape(128, NIC * NST))
    dv_r = np.ascontiguousarray(dv.reshape(NIC, 128).T)
    w_outT = np.ascontiguousarray(w_out.T)
    w1T = np.ascontiguousarray(w1.T)
    b1_r = np.ascontiguousarray(b1.reshape(NHC, 128).T)
    w2T = np.ascontiguousarray(w2.T)
    b2_r = np.ascontiguousarray(b2.reshape(NKC, 128).T)

    nc = _get_nc()
    in_maps = []
    for c in range(NCORES):
        in_maps.append(dict(
            x=np.ascontiguousarray(x[c * BPC:(c + 1) * BPC]),
            w_inT=w_inT, cw=cw, xpB=xpB, xpC=xpC, xpD=xpD, alog=alog_r,
            dvec=dv_r, w_outT=w_outT, w1T=w1T, b1=b1_r, w2T=w2T, b2=b2_r))
    res = run_bass_kernel_spmd(nc, in_maps, core_ids=list(range(NCORES)))
    out = np.concatenate([res.results[c]["out"] for c in range(NCORES)], axis=0)
    return out


# version 22
# speedup vs baseline: 1.0195x; 1.0195x over previous
import numpy as np
import ml_dtypes
from collections import defaultdict
from contextlib import ExitStack

import concourse.bass as bass
import concourse.mybir as mybir
from concourse.bass_utils import run_bass_kernel_spmd

F32 = mybir.dt.float32
BF16 = mybir.dt.bfloat16
AF = mybir.ActivationFunctionType
OP = mybir.AluOpType
AX = mybir.AxisListType

MAGIC = 12582912.0

NCORES = 8
BPC = 4
SEQ = 2048
H = 1024
I = 2048
NST = 16
HID = 4096
TT = 512
NTC = TT // 128
NTILES = BPC * (SEQ // TT)
NKC = H // 128
NIC = I // 128
NHC = HID // 128
NQ = SEQ // TT


class P:
    def __init__(self):
        self.ops = {k: [] for k in ("T", "V", "S", "Q", "G")}
        self.cnt = defaultdict(int)

    SAFE = False

    def op(self, eng, fn, deps=(), inc=0, sem=None, pre_drain=False, tag=None, rel=False):
        sem = sem or eng
        for (k, v) in deps:
            assert v <= self.cnt[k], f"dep on future token {k}:{v} (now {self.cnt[k]})"
        if self.SAFE is True or (self.SAFE and (eng in self.SAFE or (tag and tag in self.SAFE))):
            deps = tuple((k, v) for k, v in self.cnt.items() if v > 0)
        elif self.SAFE and "qT" in self.SAFE and eng == "Q" and tag:
            deps = tuple(deps) + (("T", self.cnt["T"]),) if self.cnt["T"] else deps
        tok = None
        if inc:
            self.cnt[sem] += inc
            tok = (sem, self.cnt[sem])
        self.ops[eng].append((tuple(deps), fn, inc, sem, pre_drain, rel))
        return tok


def dep(*toks):
    return tuple(t for t in toks if t is not None)


def build_program(ntiles=NTILES, safe=False, dbg=False, cut=None):
    nc = bass.Bass("TRN2", target_bir_lowering=False, detect_race_conditions=False)

    x_d = nc.dram_tensor("x", [BPC, SEQ, H], F32, kind="ExternalInput").ap()
    w_inT_d = nc.dram_tensor("w_inT", [H, 2 * I], F32, kind="ExternalInput").ap()
    cw_d = nc.dram_tensor("cw", [128, NIC * 4], F32, kind="ExternalInput").ap()
    xpB_d = nc.dram_tensor("xpB", [128, NIC * NST], F32, kind="ExternalInput").ap()
    xpC_d = nc.dram_tensor("xpC", [128, NIC * NST], F32, kind="ExternalInput").ap()
    xpD_d = nc.dram_tensor("xpD", [128, NIC], F32, kind="ExternalInput").ap()
    alog_d = nc.dram_tensor("alog", [128, NIC * NST], F32, kind="ExternalInput").ap()
    dvec_d = nc.dram_tensor("dvec", [128, NIC], F32, kind="ExternalInput").ap()
    w_outT_d = nc.dram_tensor("w_outT", [I, H], F32, kind="ExternalInput").ap()
    w1T_d = nc.dram_tensor("w1T", [H, HID], F32, kind="ExternalInput").ap()
    b1_d = nc.dram_tensor("b1", [128, NHC], F32, kind="ExternalInput").ap()
    w2T_d = nc.dram_tensor("w2T", [HID, H], F32, kind="ExternalInput").ap()
    b2_d = nc.dram_tensor("b2", [128, NKC], F32, kind="ExternalInput").ap()
    out_d = nc.dram_tensor("out", [BPC, SEQ, H], F32, kind="ExternalOutput").ap()

    s_win = nc.dram_tensor("s_win", [16, 128, NKC, 256], BF16, kind="Internal").ap()
    s_outp = nc.dram_tensor("s_outp", [8, 128, NIC, 128], BF16, kind="Internal").ap()
    s_w1 = nc.dram_tensor("s_w1", [16, 128, NKC, 256], BF16, kind="Internal").ap()
    s_w2 = nc.dram_tensor("s_w2", [4, 4, 128, 8, 256], BF16, kind="Internal").ap()

    ident_d = nc.inline_tensor(np.eye(128, dtype=ml_dtypes.bfloat16), "ident").ap()
    ones1_d = nc.inline_tensor(np.ones((1, 128), dtype=ml_dtypes.bfloat16), "ones1").ap()
    sel_np = np.zeros((16, 16, 128), dtype=ml_dtypes.bfloat16)
    for n in range(16):
        sel_np[n, n, :] = 1
    sel_d = nc.inline_tensor(sel_np.reshape(16, 16 * 128), "sel").ap()
    bias_np = np.broadcast_to(
        np.array([1e-5, 1.0, 2.772588722239781, MAGIC], dtype=np.float32), (128, 4)).copy()
    bias_d = nc.inline_tensor(bias_np, "biasc").ap()

    prog = P()
    prog.SAFE = safe
    op = prog.op
    ctx = ExitStack()
    E = ctx.enter_context
    sb = lambda name, shape, dt: E(nc.sbuf_tensor(name, shape, dt))

    x_tm = [sb(f"x_tm{i}", [128, NTC, H], BF16) for i in range(3)]
    sq_scr = sb("sq_scr", [128, 256], F32)
    q_scr = sb("q_scr", [128, H], F32)
    lntp = [sb(f"lntp{i}", [128, H], BF16) for i in range(2)]
    actF = [sb(f"actF{i}", [128, NKC, TT], BF16) for i in range(2)]
    st = [sb(f"st{i}", [128, 16], F32) for i in range(2)]
    xh_scr = [sb(f"xh{i}", [128, TT + 3], BF16) for i in range(2)]
    xc_scr = sb("xc_scr", [128, TT], BF16)
    xh_tail = sb("xh_tail", [128, NIC * 3], BF16)
    x_conv = sb("x_conv", [128, NIC, TT], BF16)
    u_sb = sb("u", [128, NIC, TT], BF16)
    g_bc = sb("g_bc", [128, NST, TT], BF16)
    delta_bc = sb("delta_bc", [128, TT], BF16)
    y_w = sb("y_w", [128, TT], F32)
    y_sb = sb("y", [128, NIC, TT], BF16)
    hq = sb("hq", [128, NHC, TT], BF16)
    o_sc = [sb(f"o_sc{i}", [128, TT], BF16) for i in range(2)]
    t_scr = [sb(f"t_scr{i}", [128, TT], F32) for i in range(2)]
    proj_sb = sb("proj_sb", [16, 3 * TT], F32)
    g_bf = sb("g_bf", [16, TT], BF16)
    delta16 = sb("delta16", [1, TT], BF16)
    winbuf = [sb(f"winbuf{i}", [128, NKC, 256], BF16) for i in range(2)]
    outpbuf = [sb(f"outpbuf{i}", [128, NIC, 128], BF16) for i in range(2)]
    w1buf = [sb(f"w1buf{i}", [128, NKC, 256], BF16) for i in range(2)]
    w2buf = [sb(f"w2buf{i}", [128, NHC // 4, 256], BF16) for i in range(2)]
    ident_sb = sb("ident_sb", [128, 128], BF16)
    ones1_sb = sb("ones1_sb", [1, 128], BF16)
    sel_sb = sb("sel_sb", [16, 16 * 128], BF16)
    cw_sb = sb("cw_sb", [128, NIC * 4], F32)
    xpB_sb = sb("xpB_sb", [128, NIC * NST], BF16)
    xpC_sb = sb("xpC_sb", [128, NIC * NST], BF16)
    xpD_sb = sb("xpD_sb", [128, NIC], BF16)
    a_sb = sb("a_sb", [128, NIC * NST], F32)
    dvec_sb = sb("dvec_sb", [128, NIC], F32)
    b1m_sb = sb("b1m_sb", [128, NHC], F32)
    b2q_sb = sb("b2q_sb", [128, NKC], F32)
    bias_sb = sb("bias_sb", [128, 4], F32)
    xp_f32 = sq_scr
    wq_in = [q_scr[:, 0:512], q_scr[:, 512:1024], t_scr[0][:], t_scr[1][:]]
    wq_out = [lntp[0][:, 0:512], lntp[0][:, 512:1024],
              lntp[1][:, 0:512], lntp[1][:, 512:1024]]

    ps = [E(nc.psum_tensor(f"ps{i}", [128, 512], F32)) for i in range(6)]
    pb = [E(nc.psum_tensor(f"pb{i}", [128, 1024], BF16)) for i in range(2)]

    sems = {k: E(nc.semaphore(f"sem_{k}")) for k in (
        "T", "V", "S", "QD", "GX", "GO", "CN", "SW", "SO", "WQ0", "WQ1", "WQ2", "WQ3",
        "WIN0", "WIN1", "OP0", "OP1", "W10", "W11", "W20", "W21",
        "GX0", "GX1", "GX2", "GO0", "GO1", "GO2")}

    ps_tok = [None] * 6
    pb_tok = [None] * 2

    for dst, src in ((ident_sb, ident_d), (ones1_sb, ones1_d), (sel_sb, sel_d),
                     (cw_sb, cw_d), (dvec_sb, dvec_d), (bias_sb, bias_d)):
        tk_const = op("G", lambda g, d=dst, s=src: g.dma_start(out=d[:], in_=s),
                      inc=16, sem="CN")
    w_inT_r = w_inT_d.rearrange("(kc p) (g2 e) -> g2 p kc e", p=128, e=256)
    for g2 in range(16):
        tk_swin = op("G", lambda g, g2=g2: g.dma_start(
            out=s_win[g2], in_=w_inT_r[g2]), inc=16, sem="SW")
    w_outT_r = w_outT_d.rearrange("(kc p) (hc e) -> hc p kc e", p=128, e=128)
    for hc in range(8):
        tk_soutp = op("G", lambda g, hc=hc: g.dma_start(
            out=s_outp[hc], in_=w_outT_r[hc]), inc=16, sem="SO")

    tk = op("Q", lambda q: q.dma_start(out=xp_f32[:, 0:NIC * NST], in_=xpB_d), inc=16, sem="QD")
    tk_xpB = op("V", lambda v: v.tensor_copy(xpB_sb[:], xp_f32[:, 0:NIC * NST]), deps=(tk,), inc=1)
    tk = op("Q", lambda q: q.dma_start(out=xp_f32[:, 0:NIC * NST], in_=xpC_d), deps=(tk_xpB,), inc=16, sem="QD")
    tk_xpC = op("V", lambda v: v.tensor_copy(xpC_sb[:], xp_f32[:, 0:NIC * NST]), deps=(tk,), inc=1)
    tk = op("Q", lambda q: q.dma_start(out=xp_f32[:, 0:NIC * NST], in_=alog_d), deps=(tk_xpC,), inc=16, sem="QD")
    tk_alog = op("S", lambda s: s.activation(a_sb[:], xp_f32[:, 0:NIC * NST], AF.Exp), deps=(tk,), inc=1)
    tk_a = op("V", lambda v: v.tensor_scalar(a_sb[:], a_sb[:], -1.0, None, OP.mult),
              deps=(tk_alog,), inc=1)
    tk = op("Q", lambda q: q.dma_start(out=xp_f32[:, 0:NIC], in_=xpD_d), deps=(tk_alog,), inc=16, sem="QD")
    tk_xpD = op("V", lambda v: v.tensor_copy(xpD_sb[:], xp_f32[:, 0:NIC]), deps=(tk,), inc=1)

    tk = op("Q", lambda q: q.dma_start(out=xp_f32[:, 0:NHC], in_=b1_d), deps=(tk_xpD,), inc=16, sem="QD")
    op("V", lambda v: v.tensor_scalar(b1m_sb[:], xp_f32[:, 0:NHC], 64.0, MAGIC, OP.mult, OP.add), deps=(tk,))
    op("V", lambda v: v.tensor_scalar(b1m_sb[:], b1m_sb[:], MAGIC + 127.0, MAGIC - 128.0, OP.min, OP.max), pre_drain=True)
    tk_b1 = op("V", lambda v: v.tensor_scalar(b1m_sb[:], b1m_sb[:], MAGIC, 0.25, OP.subtract, OP.mult), inc=1, pre_drain=True)
    tk = op("Q", lambda q: q.dma_start(out=xp_f32[:, 0:NKC], in_=b2_d), deps=(tk_b1,), inc=16, sem="QD")
    op("V", lambda v: v.tensor_scalar(b2q_sb[:], xp_f32[:, 0:NKC], 64.0, MAGIC, OP.mult, OP.add), deps=(tk,))
    op("V", lambda v: v.tensor_scalar(b2q_sb[:], b2q_sb[:], MAGIC + 127.0, MAGIC - 128.0, OP.min, OP.max), pre_drain=True)
    tk_b2 = op("V", lambda v: v.tensor_scalar(b2q_sb[:], b2q_sb[:], MAGIC, 1.0 / 64.0, OP.subtract, OP.mult), inc=1, pre_drain=True)

    wq_free = [tk_b2, tk_b2, tk_b2, tk_b2]
    st_tok = [None, None, None, None]
    quant_jobs = []
    for kc in range(NKC):
        for cc in range(8):
            quant_jobs.append((w1T_d, s_w1, kc, cc * 512))
    for kc in range(NHC):
        for cc in range(2):
            quant_jobs.append((w2T_d, s_w2, kc, cc * 512))
    for j, (srcd, dstd, kc, co) in enumerate(quant_jobs):
        bi = j % 4
        tk_l = op("Q", lambda q, s=srcd, kc=kc, co=co, b=bi:
                  q.dma_start(out=wq_in[b], in_=s[kc * 128:(kc + 1) * 128, co:co + 512]),
                  deps=dep(wq_free[bi]), inc=16, sem=f"WQ{bi}")
        tk_s1 = op("S", lambda s, b=bi: s.activation(wq_in[b], wq_in[b], AF.Identity,
                                                     scale=64.0, bias=bias_sb[:, 3:4]),
                   deps=dep(tk_l, st_tok[bi], tk_const), inc=1, rel=True)
        op("V", lambda v, b=bi: v.tensor_scalar(wq_in[b], wq_in[b], MAGIC + 127.0, MAGIC - 128.0, OP.min, OP.max),
           deps=(tk_s1,))
        tk_q = op("V", lambda v, b=bi: v.tensor_scalar(wq_out[b], wq_in[b], MAGIC, None, OP.subtract), inc=1)
        wq_free[bi] = tk_q
        if dstd is s_w1:
            dst_ap = s_w1[co // 256:co // 256 + 2, :, kc, :]
        else:
            dst_ap = s_w2[co // 256:co // 256 + 2, kc // 8, :, kc % 8, :]
        st_tok[bi] = op("G", lambda g, d=dst_ap.rearrange("bb p e -> p bb e"), b=bi:
                        g.dma_start(out=d, in_=wq_out[b]),
                        deps=(tk_q,), inc=16, sem="GX")
    tk_scratch = ("GX", prog.cnt["GX"])
    tk_p0v = ("V", prog.cnt["V"])

    store_tok = [None] * ntiles
    win_free = [None, None]
    outp_free = [None, None]
    w1_free = [None, None]
    w2_free = [None, None]
    xh_free = [None, None]
    osc_free = [None, None]
    tscr_free = [tk_p0v, tk_p0v]
    xtm_free = [None, None, None]
    tp_done_par = [None, None]
    aF_par = [tk_p0v, tk_p0v]
    ffn1_lastT_par = [None, None]
    gbc_last_v = None
    ytile_last = None
    hq_last = None
    hq_ready_prev = None
    projrd_last = None
    TS = [dict() for _ in range(ntiles)]

    EPb = [hq[:, 0:NST, :], hq[:, NST:2 * NST, :]]

    def emit_ln_stats(ti, dep0, scale16=False):
        par, pt = ti % 2, ti % 3
        stp = st[par]
        sq0 = 48 if scale16 else 0
        for tc in range(NTC):
            op("V", lambda v, p=pt, tc=tc, sq0=sq0: v.bn_stats(
                sq_scr[:, sq0 + 12 * tc:sq0 + 12 * tc + 6], x_tm[p][:, tc, 0:512]),
                deps=dep0 if tc == 0 else ())
            op("V", lambda v, p=pt, tc=tc, sq0=sq0: v.bn_stats(
                sq_scr[:, sq0 + 12 * tc + 6:sq0 + 12 * tc + 12], x_tm[p][:, tc, 512:1024]))
        for tc in range(NTC):
            tk_var = op("V", lambda v, tc=tc, stp=stp, sq0=sq0: v.bn_aggr(
                stp[:, 2 * tc:2 * tc + 2], sq_scr[:, sq0 + 12 * tc:sq0 + 12 * tc + 12]),
                pre_drain=(tc == 0), inc=1 if tc == NTC - 1 else 0)
        op("S", lambda s, stp=stp: s.activation(
            stp[:, 8:12], stp[:].rearrange("p (a b) -> p a b", b=2)[:, 0:4, 1], AF.Ln,
            bias=bias_sb[:, 0:1]), deps=dep(tk_var, tk_const if ti == 0 else None))
        if scale16:
            tk_rstd = op("S", lambda s, stp=stp: s.activation(
                stp[:, 8:12], stp[:, 8:12], AF.Exp, scale=-0.5, bias=bias_sb[:, 2:3]),
                inc=1, pre_drain=True)
        else:
            tk_rstd = op("S", lambda s, stp=stp: s.activation(
                stp[:, 8:12], stp[:, 8:12], AF.Exp, scale=-0.5), inc=1, pre_drain=True)
        return tk_rstd


    def emit_load_A(ti):
        S = TS[ti]
        seq, qq, par = ti // NQ, ti % NQ, ti % 2
        pt = ti % 3
        s0 = qq * TT
        tk_x = op("G", lambda g, b=seq, s0=s0, p=pt: g.dma_start(
            out=x_tm[p][:],
            in_=x_d[b, s0:s0 + TT, :].rearrange("(tc p) h -> p tc h", p=128)),
            deps=dep(xtm_free[pt]), inc=16, sem=f"GX{pt}")
        stp, lnp = st[par], lntp[par]
        tk_rstd = emit_ln_stats(ti, dep(tk_x, aF_par[par]))
        aF_tok = None
        for tc in range(NTC):
            tk_ln = op("V", lambda v, p=pt, tc=tc, stp=stp, lnp=lnp: v.tensor_scalar(
                lnp[:], x_tm[p][:, tc, :], stp[:, 2 * tc:2 * tc + 1], stp[:, 8 + tc:9 + tc],
                OP.subtract, OP.mult),
                deps=dep(tk_rstd, tp_done_par[par],
                         tk_scratch if ti <= 1 and tc == 0 else None), inc=1,
                pre_drain=(tc == 0))
            pbi = tc % 2
            for hc in range(NKC):
                tkt = op("T", lambda t, hc=hc, pbi=pbi, lnp=lnp: t.transpose(
                    pb[pbi][:, hc * 128:(hc + 1) * 128], lnp[:, hc * 128:(hc + 1) * 128], ident_sb[:]),
                    deps=dep(tk_ln, pb_tok[pbi], tk_const if ti == 0 else None) if hc == 0 else (),
                    inc=1 if hc == NKC - 1 else 0)
            tp_done_par[par] = tkt
            aF_tok = op("S", lambda s, tc=tc, pbi=pbi, p=par: s.copy(
                actF[p][:, :, tc * 128:(tc + 1) * 128],
                pb[pbi][:].rearrange("p (kc t) -> p kc t", kc=NKC)),
                deps=dep(tkt, ffn1_lastT_par[par] if tc == 0 else None), inc=1, rel=(tc < NTC - 1))
            pb_tok[pbi] = aF_tok
        S["aF"] = aF_tok
        S["win_tok"] = {0: _win_dma(ti, 0), 1: _win_dma(ti, 1)}
        S["mm"] = {}
        S["zev"] = {}

    def _win_dma(ti, g2):
        bi = g2 % 2
        return op("Q", lambda q, g2=g2, b=bi: q.dma_start(
            out=winbuf[b][:], in_=s_win[g2]),
            deps=dep(win_free[bi], tk_swin if ti == 0 else None), inc=16, sem=f"WIN{bi}", tag="qwin")

    def emit_B_pair(ti, g2):
        S = TS[ti]
        qq, par = ti % NQ, ti % 2
        bi = g2 % 2
        for oc2 in range(2):
            oc = 2 * g2 + oc2
            psb = oc % 3
            for kc in range(NKC):
                d = dep(S["win_tok"][g2], S["aF"], ps_tok[psb]) if kc == 0 else ()
                tkm = op("T", lambda t, b=bi, kc=kc, oc2=oc2, psb=psb, p=par: t.matmul(
                    ps[psb][:], winbuf[b][:, kc, oc2 * 128:(oc2 + 1) * 128],
                    actF[p][:, kc, :], start=(kc == 0), stop=(kc == NKC - 1)),
                    deps=d, inc=1 if kc == NKC - 1 else 0)
            S["mm"][oc] = tkm
            if oc < NIC:
                c = oc
                xp = c % 2

                def emit_head(hc):
                    if qq == 0:
                        op("V", lambda v, xp=hc % 2: v.memset(xh_scr[xp][:, 0:3], 0.0),
                           deps=dep(tk_const if ti == 0 else None))
                    else:
                        op("V", lambda v, hc=hc, xp=hc % 2: v.tensor_copy(
                            xh_scr[xp][:, 0:3], xh_tail[:, hc * 3:(hc + 1) * 3]))

                if c == 0:
                    emit_head(0)
                    emit_head(1)
                tk_ev = op("V", lambda v, xp=xp, psb=psb: v.tensor_copy(xh_scr[xp][:, 3:3 + TT], ps[psb][:]),
                           deps=dep(S["mm"][oc],
                                    TS[ti - 1]["xproj"] if (ti > 0 and c == 0) else None),
                           inc=1, rel=True)
                ps_tok[psb] = tk_ev
                op("V", lambda v, c=c, xp=xp: v.tensor_scalar(
                    xc_scr[:], xh_scr[xp][:, 0:TT], cw_sb[:, 4 * c:4 * c + 1], None, OP.mult))
                for k in range(1, 3):
                    op("V", lambda v, c=c, xp=xp, k=k: v.scalar_tensor_tensor(
                        xc_scr[:], xh_scr[xp][:, k:k + TT], cw_sb[:, 4 * c + k:4 * c + k + 1],
                        xc_scr[:], OP.mult, OP.add))
                op("V", lambda v, c=c, xp=xp: v.scalar_tensor_tensor(
                    x_conv[:, c, :], xh_scr[xp][:, 3:3 + TT], cw_sb[:, 4 * c + 3:4 * c + 4],
                    xc_scr[:], OP.mult, OP.add))
                if c + 2 < NIC:
                    emit_head(c + 2)
                op("V", lambda v, c=c, xp=xp: v.tensor_copy(
                    xh_tail[:, c * 3:(c + 1) * 3], xh_scr[xp][:, TT:TT + 3]))
            else:
                c = oc - NIC
                if oc == 31:
                    S["z_def"] = psb
                else:
                    tk_ev = op("V", lambda v, c=c, psb=psb: v.tensor_copy(u_sb[:, c, :], ps[psb][:]),
                               deps=dep(S["mm"][oc]), inc=1, rel=True)
                    ps_tok[psb] = tk_ev
                    S["zev"][c] = tk_ev
        win_free[bi] = S["mm"][2 * g2 + 1]
        if g2 + 2 < 16:
            S["win_tok"][g2 + 2] = _win_dma(ti, g2 + 2)
        if g2 == 15:
            S["inproj_lastT"] = S["mm"][31]

    def emit_silus(ti):
        S = TS[ti]
        tk = None
        for c in range(NIC):
            tk = op("S", lambda s, c=c: s.activation(x_conv[:, c, :], x_conv[:, c, :], AF.Silu),
                    deps=dep(S["bdone"], TS[ti - 1]["yv"] if ti > 0 else None) if c == 0 else (),
                    inc=1 if c == NIC - 1 else 0, rel=True)
        S["silux"] = tk
        for c in range(NIC):
            tk = op("S", lambda s, c=c: s.activation(u_sb[:, c, :], u_sb[:, c, :], AF.Silu),
                    inc=1 if c == NIC - 1 else 0, rel=True)
        S["siluu"] = tk

    def emit_umults(ti):
        S = TS[ti]
        for c in range(NIC):
            op("V", lambda v, c=c: v.tensor_tensor(
                u_sb[:, c, :], u_sb[:, c, :], x_conv[:, c, :], OP.mult),
                deps=(S["siluu"],) if c == 0 else ())

    def emit_C(ti):
        nonlocal projrd_last, gbc_last_v
        S = TS[ti]
        for c in range(NIC):
            d0 = dep(S["silux"], ps_tok[3], ps_tok[4], ps_tok[5]) if c == 0 else ()
            op("T", lambda t, c=c: t.matmul(
                ps[3][0:16, :], xpB_sb[:, c * NST:(c + 1) * NST], x_conv[:, c, :],
                start=(c == 0), stop=(c == NIC - 1)), deps=d0)
            op("T", lambda t, c=c: t.matmul(
                ps[4][0:16, :], xpC_sb[:, c * NST:(c + 1) * NST], x_conv[:, c, :],
                start=(c == 0), stop=(c == NIC - 1)))
            tk_xproj = op("T", lambda t, c=c: t.matmul(
                ps[5][0:1, :], xpD_sb[:, c:c + 1], x_conv[:, c, :],
                start=(c == 0), stop=(c == NIC - 1)), inc=1 if c == NIC - 1 else 0)
        S["xproj"] = tk_xproj
        tk_e0 = op("S", lambda s: s.copy(proj_sb[:, 0:TT], ps[3][0:16, :]),
                   deps=dep(tk_xproj, projrd_last), inc=1)
        tk_e1 = op("S", lambda s: s.copy(proj_sb[:, TT:2 * TT], ps[4][0:16, :]), inc=1)
        tk_e2 = op("S", lambda s: s.copy(proj_sb[0:1, 2 * TT:3 * TT], ps[5][0:1, :]), inc=1)
        ps_tok[3], ps_tok[4], ps_tok[5] = tk_e0, tk_e1, tk_e2
        tk_gbf = op("V", lambda v: v.tensor_tensor(g_bf[:], proj_sb[:, 0:TT], proj_sb[:, TT:2 * TT], OP.mult),
                    deps=dep(tk_e1, projrd_last), inc=1)
        tk_sp0 = op("V", lambda v: v.tensor_scalar(y_w[0:1, :], proj_sb[0:1, 2 * TT:3 * TT], 80.0, None, OP.min),
                    deps=(tk_e2,), inc=1)
        op("S", lambda s: s.activation(q_scr[0:1, 0:TT], y_w[0:1, :], AF.Exp), deps=(tk_sp0,))
        tk_d16 = op("S", lambda s: s.activation(delta16[:], q_scr[0:1, 0:TT], AF.Ln,
                                                bias=bias_sb[0:1, 1:2]), inc=1)
        tk_mm = op("T", lambda t: t.matmul(ps[3][:], ones1_sb[:], delta16[:], start=True, stop=True),
                   deps=dep(tk_d16, ps_tok[3], tk_const if ti == 0 else None), inc=1)
        tk_dbc = op("S", lambda s: s.copy(delta_bc[:], ps[3][:]), deps=(tk_mm,), inc=1, rel=True)
        ps_tok[3] = tk_dbc
        for n in range(NST):
            psb = 3 + (n + 1) % 2
            tk_mm = op("T", lambda t, n=n, psb=psb: t.matmul(
                ps[psb][:], sel_sb[:, n * 128:(n + 1) * 128], g_bf[:], start=True, stop=True),
                deps=dep(tk_gbf if n == 0 else None, ps_tok[psb]), inc=1)
            tk_gev = op("S", lambda s, n=n, psb=psb: s.copy(g_bc[:, n, :], ps[psb][:]),
                        deps=dep(tk_mm, gbc_last_v if n == 0 else None), inc=1,
                        rel=True)
            ps_tok[psb] = tk_gev
        S["gbc"] = tk_gev
        S["dbc"] = tk_dbc
        projrd_last = tk_mm
        S["ep_free"] = [hq_last, hq_last]
        S["yv"] = None

    def emit_D_chunk(ti, c):
        nonlocal gbc_last_v
        S = TS[ti]
        EP = EPb[c % 2]
        for n in range(NST):
            d = dep(S["dbc"], tk_a if ti == 0 else None, S["ep_free"][c % 2],
                    hq_ready_prev if (ti > 0 and c <= 1) else None) if n == 0 else ()
            tk_exp = op("S", lambda s, n=n, c=c, EP=EP: s.activation(
                EP[:, n, :], delta_bc[:], AF.Exp,
                scale=a_sb[:, c * NST + n:c * NST + n + 1]),
                deps=d, inc=1 if n == NST - 1 else 0, rel=True)
        S["exp%d" % c] = tk_exp
        op("V", lambda v, EP=EP: v.tensor_tensor(EP[:, :, :], EP[:, :, :], g_bc[:, :, :], OP.mult),
           deps=dep(tk_exp, S["gbc"] if c == 0 else None))
        op("V", lambda v, EP=EP: v.tensor_tensor(
            EP[:, 0:8, :], EP[:, 0:8, :], EP[:, 8:16, :], OP.add))
        op("V", lambda v, EP=EP: v.tensor_tensor(
            EP[:, 0:4, :], EP[:, 0:4, :], EP[:, 4:8, :], OP.add))
        op("V", lambda v, EP=EP: v.tensor_tensor(
            EP[:, 0:2, :], EP[:, 0:2, :], EP[:, 2:4, :], OP.add))
        op("V", lambda v, EP=EP: v.tensor_tensor(y_w[:], EP[:, 0, :], EP[:, 1, :], OP.add))
        tk_y = op("V", lambda v, c=c: v.scalar_tensor_tensor(
            y_sb[:, c, :], y_w[:], dvec_sb[:, c:c + 1], u_sb[:, c, :], OP.add, OP.mult),
            deps=dep(ytile_last if c == 0 else None), inc=1, rel=True)
        S["ep_free"][c % 2] = tk_y
        if c == NIC - 1:
            S["yv"] = tk_y
            gbc_last_v = tk_y

    def emit_D2(ti):
        nonlocal ytile_last
        S = TS[ti]
        par, pt = ti % 2, ti % 3
        outp_tok = {0: _outp_dma(ti, 0), 1: _outp_dma(ti, 1)}
        for blk in range(4):
            ev_tok = []
            for hcp in range(2):
                hc = 2 * blk + hcp
                bi = hc % 2
                psb = 3 + hcp
                for kc in range(NIC):
                    d = dep(outp_tok[hc], S["yv"], ps_tok[psb]) if kc == 0 else ()
                    tk_mm = op("T", lambda t, b=bi, kc=kc, psb=psb: t.matmul(
                        ps[psb][:], outpbuf[b][:, kc, :],
                        y_sb[:, kc, :], start=(kc == 0), stop=(kc == NIC - 1)),
                        deps=d, inc=1 if kc == NIC - 1 else 0)
                osl = hcp
                tk_ev = op("V", lambda v, osl=osl, psb=psb: v.tensor_copy(o_sc[osl][:], ps[psb][:]),
                           deps=dep(tk_mm, osc_free[osl]), inc=1, rel=(hcp == 0))
                ps_tok[psb] = tk_ev
                outp_free[bi] = tk_ev
                ev_tok.append(tk_ev)
                if hc + 2 < 8:
                    outp_tok[hc + 2] = _outp_dma(ti, hc + 2)
            if blk == 3:
                ytile_last = ps_tok[4]
            pbi = blk % 2
            for tc in range(NTC):
                for hcp in range(2):
                    tk_tp = op("T", lambda t, tc=tc, hcp=hcp, pbi=pbi: t.transpose(
                        pb[pbi][:, tc * 256 + hcp * 128: tc * 256 + (hcp + 1) * 128],
                        o_sc[hcp][:, tc * 128:(tc + 1) * 128], ident_sb[:]),
                        deps=dep(ev_tok[-1], pb_tok[pbi]) if tc == 0 else (),
                        inc=1 if (tc == NTC - 1 and hcp == 1) else 0)
            osc_free[0] = osc_free[1] = tk_tp
            for tc in range(NTC):
                add_tok = op("V", lambda v, p=pt, tc=tc, blk=blk, pbi=pbi: v.tensor_tensor(
                    x_tm[p][:, tc, blk * 256:(blk + 1) * 256],
                    x_tm[p][:, tc, blk * 256:(blk + 1) * 256],
                    pb[pbi][:, tc * 256:(tc + 1) * 256], OP.add),
                    deps=(tk_tp,) if tc == 0 else (), inc=1 if tc == NTC - 1 else 0,
                    rel=True)
            pb_tok[pbi] = add_tok
        S["attn_add"] = add_tok

    def _outp_dma(ti, hc):
        bi = hc % 2
        return op("Q", lambda q, hc=hc, b=bi: q.dma_start(
            out=outpbuf[b][:], in_=s_outp[hc]),
            deps=dep(outp_free[bi], tk_soutp if ti == 0 else None), inc=16, sem=f"OP{bi}", tag="qoutp")

    def emit_E(ti):
        S = TS[ti]
        par, pt = ti % 2, ti % 3
        stp, lnp = st[par], lntp[par]
        add_tok = S["attn_add"]
        tk_rstd = emit_ln_stats(ti, dep(add_tok), scale16=True)
        aF_tok = None
        for tc in range(NTC):
            op("V", lambda v, p=pt, tc=tc, stp=stp: v.tensor_scalar(
                q_scr[:], x_tm[p][:, tc, :], stp[:, 2 * tc:2 * tc + 1], stp[:, 8 + tc:9 + tc],
                OP.subtract, OP.mult), deps=dep(tk_rstd, tp_done_par[par]), pre_drain=(tc == 0))
            op("V", lambda v: v.tensor_scalar(q_scr[:], q_scr[:], MAGIC, MAGIC + 127.0, OP.add, OP.min))
            tk_ln = op("V", lambda v, lnp=lnp: v.tensor_scalar(
                lnp[:], q_scr[:], MAGIC - 128.0, MAGIC, OP.max, OP.subtract), inc=1)
            pbi = tc % 2
            for hc in range(NKC):
                tkt = op("T", lambda t, hc=hc, pbi=pbi, lnp=lnp: t.transpose(
                    pb[pbi][:, hc * 128:(hc + 1) * 128], lnp[:, hc * 128:(hc + 1) * 128], ident_sb[:]),
                    deps=dep(tk_ln, pb_tok[pbi]) if hc == 0 else (),
                    inc=1 if hc == NKC - 1 else 0)
            tp_done_par[par] = tkt
            aF_tok = op("S", lambda s, tc=tc, pbi=pbi, p=par: s.copy(
                actF[p][:, :, tc * 128:(tc + 1) * 128],
                pb[pbi][:].rearrange("p (kc t) -> p kc t", kc=NKC)),
                deps=dep(tkt, S["inproj_lastT"] if tc == 0 else None), inc=1, rel=(tc < NTC - 1))
            pb_tok[pbi] = aF_tok
        S["xqF"] = aF_tok

    def _w1_dma(ti, blk):
        bi = blk % 2
        return op("Q", lambda q, blk=blk, b=bi: q.dma_start(
            out=w1buf[b][:], in_=s_w1[blk]),
            deps=dep(w1_free[bi], tk_scratch if ti == 0 else None), inc=16, sem=f"W1{bi}", tag="qw1")

    def _w2_dma(ti, ocp, hh):
        bi = hh % 2
        return op("Q", lambda q, ocp=ocp, hh=hh, b=bi: q.dma_start(
            out=w2buf[b][:], in_=s_w2[ocp, hh]),
            deps=dep(w2_free[bi], tk_scratch if ti == 0 else None), inc=16, sem=f"W2{bi}", tag="qw2")

    def emit_F(ti):
        nonlocal hq_last, hq_ready_prev
        S = TS[ti]
        par, pt = ti % 2, ti % 3
        aF_tok = S["xqF"]
        w1_tok = {0: _w1_dma(ti, 0), 1: _w1_dma(ti, 1)}
        for blk in range(16):
            bi = blk % 2
            ev2 = []
            for hcp in range(2):
                hc = 2 * blk + hcp
                psb = 3 + (2 * blk + hcp) % 3
                for kc in range(NKC):
                    d = dep(w1_tok[blk], aF_tok, ps_tok[psb]) if kc == 0 else ()
                    tk_mm = op("T", lambda t, b=bi, kc=kc, hcp=hcp, psb=psb, p=par: t.matmul(
                        ps[psb][:], w1buf[b][:, kc, hcp * 128:(hcp + 1) * 128],
                        actF[p][:, kc, :], start=(kc == 0), stop=(kc == NKC - 1)),
                        deps=d, inc=1 if kc == NKC - 1 else 0)
                tk_r = op("S", lambda s, hc=hc, psb=psb, tsp=hcp: s.activation(
                    t_scr[tsp][:], ps[psb][:], AF.Relu, scale=1.0 / 64.0,
                    bias=b1m_sb[:, hc:hc + 1]),
                    deps=dep(tk_mm, tk_b1 if ti == 0 else None,
                             hq_last if (blk == 0 and hcp == 0) else None,
                             tscr_free[hcp]), inc=1, rel=True)
                op("V", lambda v, tsp=hcp: v.tensor_scalar(
                    t_scr[tsp][:], t_scr[tsp][:], MAGIC, MAGIC + 127.0, OP.add, OP.min),
                    deps=(tk_r,))
                tk_q = op("V", lambda v, hc=hc, tsp=hcp: v.tensor_scalar(
                    hq[:, hc, :], t_scr[tsp][:], MAGIC, MAGIC, OP.max, OP.subtract), inc=1,
                    rel=(blk < 15 or hcp < 1))
                tscr_free[hcp] = tk_q
                ps_tok[psb] = tk_q
                ev2.append(tk_q)
            w1_free[bi] = tk_mm
            if blk + 2 < 16:
                w1_tok[blk + 2] = _w1_dma(ti, blk + 2)
        ffn1_lastT_par[par] = tk_mm
        hq_ready = tk_q
        hq_ready_prev = tk_q

        w2_tok = {(0, hh): _w2_dma(ti, 0, hh) for hh in range(2)}
        for ocp in range(4):
            ev_tok = []
            last_mm = None
            for hh in range(4):
                bi = hh % 2
                for oc2 in range(2):
                    psb = 3 + oc2
                    for k16 in range(8):
                        hc = hh * 8 + k16
                        d = dep(w2_tok[(ocp, hh)] if (oc2 == 0 and k16 == 0) else None,
                                hq_ready if (hh == 0 and oc2 == 0 and k16 == 0) else None,
                                ps_tok[psb] if (hh == 0 and k16 == 0) else None) if k16 == 0 else ()
                        last_mm = op("T", lambda t, b=bi, k16=k16, oc2=oc2, psb=psb, hc=hc: t.matmul(
                            ps[psb][:], w2buf[b][:, k16, oc2 * 128:(oc2 + 1) * 128],
                            hq[:, hc, :], start=(hc == 0), stop=(hc == NHC - 1)),
                            deps=d,
                            inc=1 if (k16 == 7 and (oc2 == 1 or hh == 3)) else 0)
                w2_free[bi] = last_mm
                nxt = (ocp, hh + 2) if hh + 2 < 4 else ((ocp + 1, hh - 2) if ocp + 1 < 4 else None)
                if nxt is not None and nxt not in w2_tok:
                    w2_tok[nxt] = _w2_dma(ti, nxt[0], nxt[1])
            for oc2 in range(2):
                oc = 2 * ocp + oc2
                psb = 3 + oc2
                tk_ev = op("V", lambda v, oc=oc, osl=oc2, psb=psb: v.tensor_scalar(
                    o_sc[osl][:], ps[psb][:], 1.0 / 1024.0, b2q_sb[:, oc:oc + 1], OP.mult, OP.add),
                    deps=dep(last_mm, osc_free[oc2], tk_b2 if ti == 0 else None), inc=1,
                    rel=(oc2 == 0))
                ps_tok[psb] = tk_ev
                ev_tok.append(tk_ev)
            if ocp == 3:
                hq_last = ps_tok[4]
            pbi = ocp % 2
            for tc in range(NTC):
                for oc2 in range(2):
                    tk_tp = op("T", lambda t, tc=tc, oc2=oc2, pbi=pbi: t.transpose(
                        pb[pbi][:, tc * 256 + oc2 * 128: tc * 256 + (oc2 + 1) * 128],
                        o_sc[oc2][:, tc * 128:(tc + 1) * 128], ident_sb[:]),
                        deps=dep(ev_tok[-1], pb_tok[pbi]) if tc == 0 else (),
                        inc=1 if (tc == NTC - 1 and oc2 == 1) else 0)
            osc_free[0] = osc_free[1] = tk_tp
            for tc in range(NTC):
                add_tok2 = op("V", lambda v, p=pt, tc=tc, ocp=ocp, pbi=pbi: v.tensor_tensor(
                    x_tm[p][:, tc, ocp * 256:(ocp + 1) * 256],
                    x_tm[p][:, tc, ocp * 256:(ocp + 1) * 256],
                    pb[pbi][:, tc * 256:(tc + 1) * 256], OP.add),
                    deps=(tk_tp,) if tc == 0 else (), inc=1 if tc == NTC - 1 else 0,
                    rel=(ocp < 3))
            pb_tok[pbi] = add_tok2
        S["ffn_add"] = add_tok2

    def emit_store(ti):
        seq, qq, pt = ti // NQ, ti % NQ, ti % 3
        s0 = qq * TT
        store_tok[ti] = op("G", lambda g, b=seq, s0=s0, p=pt: g.dma_start(
            out=out_d[b, s0:s0 + TT, :].rearrange("(tc p) h -> p tc h", p=128),
            in_=x_tm[p][:]),
            deps=(TS[ti]["ffn_add"],), inc=16, sem=f"GO{pt}")
        xtm_free[pt] = store_tok[ti]

    for s in range(ntiles + 1):
        a, b = s, s - 1
        if b >= 0:
            emit_C(b)
        if a < ntiles:
            emit_load_A(a)
        for k in range(16):
            if a < ntiles:
                emit_B_pair(a, k)
            if b >= 0:
                emit_D_chunk(b, k)
        if a < ntiles:
            S = TS[a]
            psb = S.pop("z_def")
            tk_ev = op("V", lambda v, psb=psb: v.tensor_copy(u_sb[:, NIC - 1, :], ps[psb][:]),
                       deps=dep(S["mm"][31]), inc=1)
            ps_tok[psb] = tk_ev
            S["bdone"] = tk_ev
            emit_silus(a)
        if b >= 0:
            emit_D2(b)
        if a < ntiles:
            emit_umults(a)
        if b >= 0:
            emit_E(b)
            emit_F(b)
            emit_store(b)

    op("G", lambda g: [g.wait_ge(sems[k], prog.cnt[k]) for k in ("GO0", "GO1", "GO2", "GO") if prog.cnt[k] > 0][-1])

    prog.emitted = {k: [] for k in prog.ops}
    with nc.Block() as block:
        def emitter(letter):
            def f(eng):
                waited = defaultdict(int)
                for deps, fn, inc, sem, pre_drain, rel in prog.ops[letter]:
                    for (k, v) in deps:
                        if k == letter and k in ("T", "V", "S"):
                            continue
                        if v > waited[k]:
                            eng.wait_ge(sems[k], v)
                            waited[k] = v
                    if pre_drain:
                        eng.drain()
                    n0 = nc.n_instructions()
                    inst = fn(eng)
                    if inc:
                        if letter in ("V", "S") and not rel:
                            eng.drain().then_inc(sems[sem], inc)
                        else:
                            inst.then_inc(sems[sem], inc)
                    prog.emitted[letter].append((n0, nc.n_instructions()))
            return f
        block.tensor(emitter("T"))
        block.vector(emitter("V"))
        block.scalar(emitter("S"))
        block.sync(emitter("Q"))
        block.gpsimd(emitter("G"))

    ctx.close()
    nc._prog = prog
    return nc


_prog_cache = {}


def _get_nc():
    if "nc" not in _prog_cache:
        _prog_cache["nc"] = build_program()
    return _prog_cache["nc"]


def kernel(L0_attn_x, L0_attn_in_proj_w, L0_attn_conv_w, L0_attn_x_proj_w,
           L0_attn_A_log, L0_attn_D, L0_attn_out_proj_w,
           L0_ffn_w1, L0_ffn_b1, L0_ffn_w2, L0_ffn_b2):
    x = np.ascontiguousarray(np.asarray(L0_attn_x, dtype=np.float32))
    w_in = np.asarray(L0_attn_in_proj_w, dtype=np.float32)
    conv_w = np.asarray(L0_attn_conv_w, dtype=np.float32)
    xp = np.asarray(L0_attn_x_proj_w, dtype=np.float32)
    alog = np.asarray(L0_attn_A_log, dtype=np.float32)
    dv = np.asarray(L0_attn_D, dtype=np.float32)
    w_out = np.asarray(L0_attn_out_proj_w, dtype=np.float32)
    w1 = np.asarray(L0_ffn_w1, dtype=np.float32)
    b1 = np.asarray(L0_ffn_b1, dtype=np.float32)
    w2 = np.asarray(L0_ffn_w2, dtype=np.float32)
    b2 = np.asarray(L0_ffn_b2, dtype=np.float32)

    w_inT = np.ascontiguousarray(w_in.T)
    cw = np.ascontiguousarray(conv_w[:, 0, :].reshape(NIC, 128, 4).transpose(1, 0, 2).reshape(128, NIC * 4))
    xpB = np.ascontiguousarray(xp[1:17].T.reshape(NIC, 128, NST).transpose(1, 0, 2).reshape(128, NIC * NST))
    xpC = np.ascontiguousarray(xp[17:33].T.reshape(NIC, 128, NST).transpose(1, 0, 2).reshape(128, NIC * NST))
    xpD = np.ascontiguousarray(xp[0].reshape(NIC, 128).T)
    alog_r = np.ascontiguousarray(alog.reshape(NIC, 128, NST).transpose(1, 0, 2).reshape(128, NIC * NST))
    dv_r = np.ascontiguousarray(dv.reshape(NIC, 128).T)
    w_outT = np.ascontiguousarray(w_out.T)
    w1T = np.ascontiguousarray(w1.T)
    b1_r = np.ascontiguousarray(b1.reshape(NHC, 128).T)
    w2T = np.ascontiguousarray(w2.T)
    b2_r = np.ascontiguousarray(b2.reshape(NKC, 128).T)

    nc = _get_nc()
    in_maps = []
    for c in range(NCORES):
        in_maps.append(dict(
            x=np.ascontiguousarray(x[c * BPC:(c + 1) * BPC]),
            w_inT=w_inT, cw=cw, xpB=xpB, xpC=xpC, xpD=xpD, alog=alog_r,
            dvec=dv_r, w_outT=w_outT, w1T=w1T, b1=b1_r, w2T=w2T, b2=b2_r))
    res = run_bass_kernel_spmd(nc, in_maps, core_ids=list(range(NCORES)))
    out = np.concatenate([res.results[c]["out"] for c in range(NCORES)], axis=0)
    return out
